# version 01 (final)
# baseline (speedup 1.0000x reference)
import math
import numpy as np
import ml_dtypes
from contextlib import ExitStack

import concourse.bass as bass
import concourse.mybir as mybir
from concourse.bass_utils import run_bass_kernel_spmd

F32 = mybir.dt.float32
BF16 = mybir.dt.bfloat16
I32 = mybir.dt.int32
AF = mybir.ActivationFunctionType
ALU = mybir.AluOpType
AX = mybir.AxisListType

BF = ml_dtypes.bfloat16

NCORE = 8
T = 2048
D = 2048
NFT = D // 128
NH, HD = 16, 128
FI = 8192
NJI = FI // 128
DI, DXB, DST, DTR = 4096, 1024, 16, 128
NDT = DI // 128
TC = 512
NCH = T // TC
EPS = 1e-5

DEV_STAGE = 4
DEV_TAP = None



def _wstream(W, name=None):
    FO, FIN = W.shape
    a = W.reshape(FO // 128, 128, FIN // 128, 128)
    return np.ascontiguousarray(a.transpose(0, 3, 2, 1))


def _rhs_stream(WT):
    FIN, FO = WT.shape
    return np.ascontiguousarray(WT.reshape(FIN // 128, 128, FO))


def _percol(v, ntile):
    return np.ascontiguousarray(v.reshape(ntile, 128).T)


def _prep(inputs):
    g = {}
    f32 = np.float32

    for nm, w in [("wq", "L0_attn_a_Wq"), ("wk", "L0_attn_a_Wk")]:
        g[nm] = _wstream(np.asarray(inputs[w], f32))
    g["wv"] = _rhs_stream(np.ascontiguousarray(np.asarray(inputs["L0_attn_a_Wv"], f32).T))
    g["wo"] = _wstream(np.asarray(inputs["L0_attn_a_Wo"], f32))
    g["bq"] = _percol(np.asarray(inputs["L0_attn_a_bq"], f32), NFT)
    g["bk"] = _percol(np.asarray(inputs["L0_attn_a_bk"], f32), NFT)
    g["bo"] = _percol(np.asarray(inputs["L0_attn_a_bo"], f32), NFT)
    g["bv"] = np.ascontiguousarray(np.asarray(inputs["L0_attn_a_bv"], f32).reshape(1, D))

    g["gw"] = _wstream(np.asarray(inputs["L0_ffn_gate_w"], np.int32))
    g["uw"] = _wstream(np.asarray(inputs["L0_ffn_up_w"], np.int32))
    g["dw"] = _wstream(np.asarray(inputs["L0_ffn_down_w"], np.int32))
    g["gs"] = _percol(np.asarray(inputs["L0_ffn_gate_scale"], f32), NJI)
    g["us"] = _percol(np.asarray(inputs["L0_ffn_up_scale"], f32), NJI)
    g["ds"] = _percol(np.asarray(inputs["L0_ffn_down_scale"], f32), NFT)

    g["ip"] = _wstream(np.asarray(inputs["L0_attn_b_in_proj_w"], f32))
    g["op"] = _wstream(np.asarray(inputs["L0_attn_b_out_proj_w"], f32))
    g["dtw"] = np.ascontiguousarray(np.asarray(inputs["L0_attn_b_dt_proj_w"], f32).T)
    g["dtb"] = _percol(np.asarray(inputs["L0_attn_b_dt_proj_b"], f32), NDT)
    g["cw"] = np.ascontiguousarray(
        np.asarray(inputs["L0_attn_b_conv_w"], f32).reshape(DI, 4).reshape(NDT, 128, 4).transpose(1, 0, 2).reshape(128, NDT * 4))
    g["cb"] = _percol(np.asarray(inputs["L0_attn_b_conv_b"], f32), NDT)
    g["alog"] = np.ascontiguousarray(
        np.asarray(inputs["L0_attn_b_A_log"], f32).reshape(NDT, 128, DST).transpose(1, 0, 2).reshape(128, NDT * DST))
    g["dpar"] = _percol(np.asarray(inputs["L0_attn_b_D_param"], f32), NDT)
    g["ln1"] = _percol(np.asarray(inputs["L0_attn_b_ln1_w"], f32), NFT)
    g["ln2"] = _percol(np.asarray(inputs["L0_attn_b_ln2_w"], f32), NFT)
    g["mg"] = _wstream(np.asarray(inputs["L0_attn_b_gate_w"], f32))
    g["mu"] = _wstream(np.asarray(inputs["L0_attn_b_up_w"], f32))
    g["md"] = _wstream(np.asarray(inputs["L0_attn_b_down_w"], f32))

    masks = np.zeros((4, 128, TC), BF)
    jj = np.arange(TC)[None, :]
    pp = np.arange(128)[:, None]
    for o in range(4):
        masks[o] = (jj >= o * 128 + pp).astype(BF)
    g["masks"] = np.ascontiguousarray(masks.transpose(1, 0, 2).reshape(128, 4 * TC))

    patP = np.zeros((128, 16 * 128), BF)
    for n in range(16):
        for h in range(8):
            for gg in range(16):
                patP[h * 16 + n, n * 128 + h * 16 + gg] = 1
    g["patP"] = patP

    patB = np.zeros((128, 64 * 128), BF)
    for n in range(16):
        for o in range(4):
            col0 = (n * 4 + o) * 128
            for hl in range(8):
                src = (2 * o + hl // 4) * 16 + n
                for gg in range(16):
                    patB[src, col0 + hl * 16 + gg] = 1
    g["patB"] = patB

    patQ = np.zeros((128, 4 * 128), BF)
    for o in range(4):
        for hl in range(8):
            for gg in range(16):
                patQ[(2 * o + hl // 4) * 16 + gg, o * 128 + hl * 16 + gg] = 1
    g["patQ"] = patQ

    g["ones_col"] = np.ones((128, 1), BF)
    g["ones_row"] = np.ones((1, 128), f32)
    cv = np.zeros((128, 8), f32)
    cv[:, 0] = EPS
    cv[:, 1] = 1.0 / math.sqrt(HD)
    cv[:, 2] = 1.0 / D
    cv[:, 3] = 1.0
    g["cvals"] = cv

    x = np.asarray(inputs["L0_attn_a_hidden_states"], f32)
    xs = [np.ascontiguousarray(x[c].T) for c in range(NCORE)]
    return g, xs



class MKB:
    ENGS = ("T", "V", "S", "G", "Q")

    NLANE = 8

    def __init__(self, nc):
        self.nc = nc
        self.ops = {e: [] for e in self.ENGS}
        self.sig = {e: 0 for e in ("T", "V", "S", "G")}
        self.hw = {e: {} for e in self.ENGS}
        self.sems = {}
        self.lane_uses = {}
        self.lane_rr = {"Q": 0, "G": 0}

    def init_sems(self, es):
        for name in ("T", "V", "S", "G"):
            self.sems[name] = es.enter_context(self.nc.semaphore("sem" + name))
        for q in ("Q", "G"):
            for i in range(self.NLANE):
                ln = f"D{q}{i}"
                self.sems[ln] = es.enter_context(self.nc.semaphore("sem" + ln))
                self.lane_uses[ln] = 0

    def _wait(self, e, tok):
        if tok is None:
            return
        sn, val = tok
        if val <= self.hw[e].get(sn, 0):
            return
        self.hw[e][sn] = val
        sem = self.sems[sn]
        self.ops[e].append(lambda eng, sem=sem, val=val: eng.wait_ge(sem, val))

    def op(self, e, fn, waits=(), sig=False):
        for t in waits:
            self._wait(e, t)
        if sig:
            self.sig[e] += 1
            cnt = self.sig[e]
            sem = self.sems[e]
            self.ops[e].append(lambda eng, fn=fn, sem=sem: fn(eng).then_inc(sem, 1))
            return (e, cnt)
        self.ops[e].append(fn)
        return None

    def dma(self, q, out, in_, waits=()):
        ln = f"D{q}{self.lane_rr[q]}"
        self.lane_rr[q] = (self.lane_rr[q] + 1) % self.NLANE
        self._wait(q, (ln, 16 * self.lane_uses[ln]))
        for t in waits:
            self._wait(q, t)
        self.lane_uses[ln] += 1
        val = 16 * self.lane_uses[ln]
        sem = self.sems[ln]
        self.ops[q].append(lambda eng, out=out, in_=in_, sem=sem:
                           eng.dma_start(out=out, in_=in_).then_inc(sem, 16))
        return (ln, val)

    def now(self):
        toks = [(e, self.sig[e]) for e in ("T", "V", "S", "G") if self.sig[e]]
        toks += [(ln, 16 * n) for ln, n in self.lane_uses.items() if n]
        return toks

    def barrier(self):
        toks = self.now()
        for e in self.ENGS:
            for t in toks:
                self._wait(e, t)

    def flush(self, name):
        nc = self.nc
        ops = {e: list(self.ops[e]) for e in self.ENGS}
        for e in self.ENGS:
            self.ops[e] = []
        with nc.Block(name) as blk:
            @blk.tensor
            def _(eng, ops=ops["T"]):
                for f in ops:
                    f(eng)

            @blk.vector
            def _(eng, ops=ops["V"]):
                for f in ops:
                    f(eng)

            @blk.scalar
            def _(eng, ops=ops["S"]):
                for f in ops:
                    f(eng)

            @blk.gpsimd
            def _(eng, ops=ops["G"]):
                for f in ops:
                    f(eng)

            @blk.sync
            def _(eng, ops=ops["Q"]):
                for f in ops:
                    f(eng)

    def mm_group(self, out_ap, items, first_waits=(), sig=True, bank_tok=None):
        n = len(items)
        tok = None
        for k, (l, r, w) in enumerate(items):
            waits = list(w)
            if k == 0:
                waits += list(first_waits)
                if bank_tok is not None:
                    waits.append(bank_tok)
            st, sp = (k == 0), (k == n - 1)
            tok = self.op("T",
                          lambda eng, l=l, r=r, st=st, sp=sp, o=out_ap:
                          eng.matmul(o, l, r, start=st, stop=sp),
                          waits=waits, sig=(sp and sig))
        return tok


def build_program():
    nc = bass.Bass("TRN2", target_bir_lowering=False, debug=False)
    mk = MKB(nc)

    dt_f, dt_b, dt_i = F32, BF16, I32

    dr = {}

    def din(name, shape, dtype):
        dr[name] = nc.dram_tensor(name, list(shape), dtype, kind="ExternalInput").ap()
        return dr[name]

    def dscr(name, shape, dtype):
        dr[name] = nc.dram_tensor(name, list(shape), dtype, kind="Internal").ap()
        return dr[name]

    x_in = din("x", (D, T), F32)
    y_out = nc.dram_tensor("y", [D, T], F32, kind="ExternalOutput").ap()

    din("wq", (NH, 128, NFT, 128), F32)
    din("wk", (NH, 128, NFT, 128), F32)
    din("wv", (NFT, 128, D), F32)
    din("wo", (NFT, 128, NFT, 128), F32)
    din("bq", (128, NFT), F32)
    din("bk", (128, NFT), F32)
    din("bo", (128, NFT), F32)
    din("bv", (1, D), F32)
    din("gw", (NJI, 128, NFT, 128), I32)
    din("uw", (NJI, 128, NFT, 128), I32)
    din("dw", (NFT, 128, NJI, 128), I32)
    din("gs", (128, NJI), F32)
    din("us", (128, NJI), F32)
    din("ds", (128, NFT), F32)
    din("ip", (81, 128, NFT, 128), F32)
    din("op", (NFT, 128, NDT, 128), F32)
    din("dtw", (128, NDT * 128), F32)
    din("dtb", (128, NDT), F32)
    din("cw", (128, NDT * 4), F32)
    din("cb", (128, NDT), F32)
    din("alog", (128, NDT * DST), F32)
    din("dpar", (128, NDT), F32)
    din("ln1", (128, NFT), F32)
    din("ln2", (128, NFT), F32)
    din("mg", (NJI, 128, NFT, 128), F32)
    din("mu", (NJI, 128, NFT, 128), F32)
    din("md", (NFT, 128, NJI, 128), F32)
    din("masks", (128, 4 * TC), BF16)
    din("patP", (128, 16 * 128), BF16)
    din("patB", (128, 64 * 128), BF16)
    din("patQ", (128, 4 * 128), BF16)
    din("ones_col", (128, 1), BF16)
    din("ones_row", (1, 128), F32)
    din("cvals", (128, 8), F32)

    x1_d = dscr("x1_d", (D, T), F32)
    x2_d = dscr("x2_d", (D, T), F32)
    x3_d = dscr("x3_d", (D, T), F32)
    o_d = dscr("o_d", (D, T), BF16)
    hf_d = dscr("hf_d", (FI, T), BF16)
    lnx2_d = dscr("lnx2_d", (D, T), F32)
    rn1_d = dscr("rn1_d", (D, T), BF16)
    hm_d = dscr("hm_d", (D, T), F32)
    rn2_d = dscr("rn2_d", (D, T), BF16)
    z_d = dscr("z_d", (DI, T), BF16)
    c_d = dscr("c_d", (DI, T), BF16)

    es = ExitStack()
    with es:
        mk.init_sems(es)

        def sb(name, shape, dtype):
            return es.enter_context(nc.sbuf_tensor(name, list(shape), dtype))

        def psb(name):
            return es.enter_context(nc.psum_tensor(name, [128, 512], F32))

        LNX = sb("LNX", (128, NFT * T), BF16)
        ONESC = sb("ones_c", (128, 1), BF16)
        ONESR = sb("ones_r", (1, 128), F32)
        ROWS = sb("rows", (1, 4 * TC), F32)
        CVAL = sb("cval", (128, 8), F32)
        TMPF = sb("tmpf", (128, 2 * TC), F32)
        RECR = sb("recr", (128, TC), F32)

        PS = [psb(f"ps{i}") for i in range(8)]
        bank_free = [None] * 8

        def LNXs(ft):
            return LNX[:, ft * T:(ft + 1) * T]

        t_c = []
        t_c.append(mk.dma("Q", ONESC[:], dr["ones_col"]))
        t_c.append(mk.dma("Q", ONESR[:], dr["ones_row"]))
        t_c.append(mk.dma("Q", CVAL[:], dr["cvals"]))
        tok_consts = t_c

        def ln_pass(src, XCH, XBF, XSQ, extras=None):
            xch_war = [[] for _ in range(NFT)]
            xbf_war = [None, None]
            xsq_war = [None, None]
            row_war = [None, None]
            for tc in range(NCH):
                cs = slice(tc * TC, (tc + 1) * TC)
                tx = []
                for ft in range(NFT):
                    tx.append(mk.dma("Q", XCH[ft], src[ft * 128:(ft + 1) * 128, cs],
                                     waits=xch_war[ft]))
                    xch_war[ft] = []
                tok_mean = None
                tok_var = None
                for ft in range(NFT):
                    pb = XBF[ft % 2]
                    tb = mk.op("V", lambda e, o=pb, i=XCH[ft]: e.tensor_copy(o, i),
                               waits=[tx[ft], xbf_war[ft % 2]], sig=True)
                    tok_mean = mk.op("T", lambda e, r=pb, st=(ft == 0), sp=(ft == NFT - 1):
                                     e.matmul(PS[0][0:1, :], ONESC[:], r, start=st, stop=sp),
                                     waits=[tb] + ([bank_free[0]] if ft == 0 else []),
                                     sig=True)
                    xbf_war[ft % 2] = tok_mean
                    sq = XSQ[ft % 2]
                    ts = mk.op("S", lambda e, o=sq, i=XCH[ft]: e.activation(o, i, AF.Square),
                               waits=[tx[ft], xsq_war[ft % 2]], sig=True)
                    tok_var = mk.op("T", lambda e, r=sq, st=(ft == 0), sp=(ft == NFT - 1):
                                    e.matmul(PS[1][0:1, :], ONESC[:], r, start=st, stop=sp),
                                    waits=[ts] + ([bank_free[1]] if ft == 0 else []),
                                    sig=True)
                    xsq_war[ft % 2] = tok_var
                mrow = ROWS[0:1, 0:TC]
                vrow = ROWS[0:1, TC:2 * TC]
                trow = ROWS[0:1, 2 * TC:3 * TC]
                t1 = mk.op("S", lambda e: e.activation(mrow, PS[0][0:1, :], AF.Copy, scale=CVAL[0:1, 2:3]),
                           waits=[tok_mean, row_war[0]], sig=True)
                t2 = mk.op("S", lambda e: e.activation(vrow, PS[1][0:1, :], AF.Copy, scale=CVAL[0:1, 2:3]),
                           waits=[tok_var, row_war[1]], sig=True)
                t3 = mk.op("V", lambda e: e.tensor_mul(trow, mrow, mrow), waits=[t1], sig=True)
                t4 = mk.op("V", lambda e: e.tensor_sub(vrow, vrow, trow), waits=[t2, t3], sig=True)
                t5 = mk.op("S", lambda e: e.activation(trow, vrow, AF.Sqrt, bias=CVAL[0:1, 0:1]),
                           waits=[t4], sig=True)
                t6 = mk.op("V", lambda e: e.reciprocal(vrow, trow), waits=[t5], sig=True)
                tm = mk.op("T", lambda e: e.matmul(PS[2][:, 0:TC], ONESR[:], mrow, start=True, stop=True),
                           waits=[t1, t3, bank_free[2]], sig=True)
                tr = mk.op("T", lambda e: e.matmul(PS[3][:, 0:TC], ONESR[:], vrow, start=True, stop=True),
                           waits=[t6, bank_free[3]], sig=True)
                tok_n = None
                for ft in range(NFT):
                    tf = TMPF[:, (ft % 2) * TC:(ft % 2 + 1) * TC]
                    ta = mk.op("V", lambda e, o=tf, i=XCH[ft]: e.tensor_sub(o, i, PS[2][:, 0:TC]),
                               waits=[tm, tx[ft]], sig=True)
                    xch_war[ft] = [ta, tok_mean, tok_var]
                    tok_n = mk.op("V", lambda e, o=LNXs(ft)[:, cs], i=tf:
                                  e.tensor_tensor(o, i, PS[3][:, 0:TC], ALU.mult),
                                  waits=[tr, ta], sig=True)
                bank_free[0] = t1
                bank_free[1] = t2
                bank_free[2] = tok_n
                bank_free[3] = tok_n
                row_war[0] = tm
                row_war[1] = tr
            return mk.now()

        with ExitStack() as ph:
            XCH = [ph.enter_context(nc.sbuf_tensor(f"xch{i}", [128, TC], F32)) for i in range(NFT)]
            XBF = [ph.enter_context(nc.sbuf_tensor(f"xbf{i}", [128, TC], BF16)) for i in range(2)]
            XSQ = [ph.enter_context(nc.sbuf_tensor(f"xsq{i}", [128, TC], BF16)) for i in range(2)]
            VH = ph.enter_context(nc.sbuf_tensor("vh", [128, NFT * 1024], BF16))
            WVS = [ph.enter_context(nc.sbuf_tensor(f"wvs{i}", [128, TC], BF16)) for i in range(NFT)]
            QH = ph.enter_context(nc.sbuf_tensor("qh", [128, 2 * T], BF16))
            KH = ph.enter_context(nc.sbuf_tensor("kh", [128, 2 * T], BF16))
            WQS = [ph.enter_context(nc.sbuf_tensor("wqs0", [128, D], BF16))] * 2
            WKS = [ph.enter_context(nc.sbuf_tensor("wks0", [128, D], BF16))] * 2
            EXPS = ph.enter_context(nc.sbuf_tensor("exps", [128, 2 * TC], BF16))
            OH = ph.enter_context(nc.sbuf_tensor("oh", [128, T], BF16))
            CONST_MASK = ph.enter_context(nc.sbuf_tensor("cmask", [128, 4 * TC], BF16))
            BQ = ph.enter_context(nc.sbuf_tensor("bq_s", [128, NFT], F32))
            BK = ph.enter_context(nc.sbuf_tensor("bk_s", [128, NFT], F32))
            BVR = ph.enter_context(nc.sbuf_tensor("bvr", [128, D], F32))

            def MASK(o):
                return CONST_MASK[:, o * TC:(o + 1) * TC]

            mk.dma("Q", CONST_MASK[:], dr["masks"])
            mk.dma("Q", BQ[:], dr["bq"])
            mk.dma("Q", BK[:], dr["bk"])
            mk.dma("Q", BVR[:], dr["bv"].broadcast_to([128, D]))
            mk.barrier()
            lnx_toks = ln_pass(x_in, [x[:] for x in XCH], [x[:] for x in XBF], [x[:] for x in XSQ])
            tok_lnx = mk.now()
            if DEV_TAP == "lnx":
                mk.barrier()
                for ft in range(NFT):
                    mk.dma("Q", y_out[ft * 128:(ft + 1) * 128, :], LNXs(ft))
                mk.barrier()
                mk.flush("tap_lnx")
                return nc

            wv_slot_tok = [None] * NFT
            vh_war = None
            qkw_war = None
            exps_war = [None, None]
            qk_war = [None, None]
            oh_war = None
            for half in range(2):
                v_toks = []
                for dvq in range(2):
                    dv0 = half * 1024 + dvq * 512
                    wv_t = []
                    for ft in range(NFT):
                        wv_t.append(mk.dma("G", WVS[ft][:], dr["wv"][ft, :, dv0:dv0 + 512],
                                           waits=[wv_slot_tok[ft]] if wv_slot_tok[ft] else []))
                    for tt in range(NFT):
                        bk = 4 + (tt % 2)
                        items = [(LNXs(ft)[:, tt * 128:(tt + 1) * 128], WVS[ft][:], [wv_t[ft]] + ([t for t in tok_lnx] if tt == 0 and ft == 0 else []))
                                 for ft in range(NFT)]
                        gtok = mk.mm_group(PS[bk][:], items, bank_tok=bank_free[bk])
                        dst = VH[:, tt * 1024 + dvq * 512:tt * 1024 + (dvq + 1) * 512]
                        etok = mk.op("V", lambda e, d=dst, b=bk, c0=dv0:
                                     e.tensor_tensor(d, PS[b][:], BVR[:, c0:c0 + 512], ALU.add),
                                     waits=[gtok, vh_war], sig=True)
                        bank_free[bk] = etok
                        v_toks.append(etok)
                    wv_slot_tok = [v_toks[-1]] * NFT
                vtok = v_toks[-1]

                if DEV_TAP == "v" and half == 0:
                    mk.barrier()
                    for s in range(8):
                        mk.dma("Q", y_out[s * 128:(s + 1) * 128, :], VH[:, s * 2048:(s + 1) * 2048])
                    mk.barrier()
                    mk.flush("tap_v")
                    return nc
                for h in range(half * 8, half * 8 + 8):
                    par = h % 2
                    twq = mk.dma("G", WQS[par][:], dr["wq"][h].rearrange("p a b -> p (a b)"),
                                 waits=[qk_war[par]] if qk_war[par] else [])
                    twk = mk.dma("G", WKS[par][:], dr["wk"][h].rearrange("p a b -> p (a b)"),
                                 waits=[qk_war[par]] if qk_war[par] else [])
                    qsl = QH[:, par * T:(par + 1) * T]
                    ksl = KH[:, par * T:(par + 1) * T]
                    tq = {}
                    for tc in range(NCH):
                        for (wsb, twx, dst, bias, bk) in [
                                (WQS[par], twq, qsl, BQ, 6), (WKS[par], twk, ksl, BK, 7)]:
                            items = [(wsb[:, ft * 128:(ft + 1) * 128],
                                      LNXs(ft)[:, tc * TC:(tc + 1) * TC], [twx]) for ft in range(NFT)]
                            gtok = mk.mm_group(PS[bk][:], items, bank_tok=bank_free[bk],
                                               first_waits=[qk_war[par]] if (tc == 0 and qk_war[par]) else [])
                            etok = mk.op("V", lambda e, d=dst[:, tc * TC:(tc + 1) * TC], b=bk, bi=bias, hh=h:
                                         e.tensor_scalar(d, PS[b][:], bi[:, hh:hh + 1], None, ALU.add),
                                         waits=[gtok] + ([qk_war[par]] if tc == 0 else []), sig=True)
                            bank_free[bk] = etok
                            tq[(tc, bk)] = etok
                    tok_q = [tq[(tc, 6)] for tc in range(NCH)]
                    tok_k = [tq[(tc, 7)] for tc in range(NCH)]
                    qkw_war = tq[(NCH - 1, 7)]
                    if DEV_TAP == "qk" and h == DEV_TAP_HEAD:
                        mk.barrier()
                        mk.dma("Q", y_out[0:128, :], qsl)
                        mk.dma("Q", y_out[128:256, :], ksl)
                        mk.barrier()
                        mk.flush("tap_qk")
                        return nc

                    hloc = h - half * 8
                    av_last = None
                    for qc in range(NCH):
                        live = 4 * qc + 4
                        den_tok = None
                        for kt in range(live):
                            bk = kt % 2
                            stok = mk.op("T", lambda e, b=bk, kk=kt, q=qc, ks=ksl, qs=qsl:
                                         e.matmul(PS[b][:], ks[:, kk * 128:(kk + 1) * 128],
                                                  qs[:, q * TC:(q + 1) * TC], start=True, stop=True),
                                         waits=[tok_k[kt // 4], tok_q[qc], bank_free[bk]], sig=True)
                            esl = EXPS[:, (kt % 2) * TC:(kt % 2 + 1) * TC]
                            etok = mk.op("S", lambda e, d=esl, b=bk:
                                         e.activation(d, PS[b][:], AF.Exp, scale=CVAL[:, 1:2]),
                                         waits=[stok, exps_war[kt % 2]], sig=True)
                            bank_free[bk] = etok
                            mtok = etok
                            if kt >= 4 * qc:
                                mtok = mk.op("V", lambda e, d=esl, o=kt - 4 * qc:
                                             e.tensor_mul(d, d, MASK(o)), waits=[etok], sig=True)
                            den_tok = mk.op("T", lambda e, r=esl, st=(kt == 0), sp=(kt == live - 1):
                                            e.matmul(PS[2][0:1, :], ONESC[:], r, start=st, stop=sp),
                                            waits=[mtok] + ([bank_free[2]] if kt == 0 else []),
                                            sig=(kt == live - 1))
                            av_last = mk.op("T", lambda e, r=esl, kk=kt, hl=hloc, st=(kt == 0), sp=(kt == live - 1):
                                            e.matmul(PS[3][:], VH[:, kk * 1024 + hl * 128:kk * 1024 + (hl + 1) * 128],
                                                     r, start=st, stop=sp),
                                            waits=[mtok, vtok] + ([bank_free[3]] if kt == 0 else []),
                                            sig=True)
                            exps_war[kt % 2] = av_last
                        drow = ROWS[0:1, 3 * TC:4 * TC]
                        t7 = mk.op("V", lambda e: e.reciprocal(drow, PS[2][0:1, :]),
                                   waits=[den_tok], sig=True)
                        bank_free[2] = t7
                        t8 = mk.op("T", lambda e: e.matmul(PS[5][:], ONESR[:], drow, start=True, stop=True),
                                   waits=[t7, bank_free[5]], sig=True)
                        t9 = mk.op("V", lambda e: e.tensor_copy(RECR[:], PS[5][:]),
                                   waits=[t8], sig=True)
                        bank_free[5] = t9
                        otok = mk.op("V", lambda e, q=qc:
                                     e.tensor_tensor(OH[:, q * TC:(q + 1) * TC],
                                                     PS[3][:], RECR[:], ALU.mult),
                                     waits=[av_last, t9, oh_war], sig=True)
                        bank_free[3] = otok

                    qk_war[par] = otok
                    vh_war = otok
                    oh_war = mk.dma("G", o_d[h * 128:(h + 1) * 128, :], OH[:], waits=[otok])

            tok_o_done = mk.now()
            if DEV_TAP == "oh":
                mk.barrier()
                war2 = [None, None]
                for i, hh in enumerate([2, 3, 4]):
                    for c in range(4):
                        par = c % 2
                        esl = EXPS[:, par * TC:(par + 1) * TC]
                        tl = mk.dma("Q", esl, o_d[hh * 128:(hh + 1) * 128, c * TC:(c + 1) * TC],
                                    waits=[war2[par]] if war2[par] else [])
                        war2[par] = mk.dma("Q", y_out[i * 128:(i + 1) * 128, c * TC:(c + 1) * TC], esl, waits=[tl])
                mk.barrier()
                mk.flush("tap_oh_end")
                return nc
            mk.barrier()
            mk.flush("phaseA_attn")

        with ExitStack() as ph:
            OS = [ph.enter_context(nc.sbuf_tensor(f"os{i}", [128, TC], BF16)) for i in range(NFT)]
            XT = [ph.enter_context(nc.sbuf_tensor(f"xt{i}", [128, TC], F32)) for i in range(NFT)]
            WOS = [ph.enter_context(nc.sbuf_tensor(f"wos{i}", [128, D], BF16)) for i in range(2)]
            BO_ = ph.enter_context(nc.sbuf_tensor("bo_s", [128, NFT], F32))
            T1 = [ph.enter_context(nc.sbuf_tensor(f"t1_{i}", [128, TC], F32)) for i in range(2)]
            X1T = [ph.enter_context(nc.sbuf_tensor(f"x1t{i}", [128, TC], F32)) for i in range(2)]

            mk.dma("Q", BO_[:], dr["bo"])
            mk.barrier()
            wos_war = [None, None]
            t1_war = [None, None]
            for tc in range(NCH):
                cs = slice(tc * TC, (tc + 1) * TC)
                to = [mk.dma("Q", OS[dvt][:], o_d[dvt * 128:(dvt + 1) * 128, cs]) for dvt in range(NFT)]
                txr = [mk.dma("Q", XT[ft][:], x_in[ft * 128:(ft + 1) * 128, cs]) for ft in range(NFT)]
                for ff in range(NFT):
                    par = ff % 2
                    tw = mk.dma("G", WOS[par][:], dr["wo"][ff].rearrange("p a b -> p (a b)"),
                                waits=[wos_war[par]] if wos_war[par] else [])
                    bk = 4 + par
                    items = [(WOS[par][:, dvt * 128:(dvt + 1) * 128], OS[dvt][:], [tw, to[dvt]])
                             for dvt in range(NFT)]
                    gtok = mk.mm_group(PS[bk][:], items, bank_tok=bank_free[bk])
                    wos_war[par] = gtok
                    tt1 = mk.op("V", lambda e, o=T1[par][:], b=bk, f=ff:
                                e.tensor_scalar(o, PS[b][:], BO_[:, f:f + 1], None, ALU.add),
                                waits=[gtok, t1_war[par]], sig=True)
                    bank_free[bk] = tt1
                    tx1 = mk.op("V", lambda e, o=X1T[par][:], a=T1[par][:], x=XT[ff][:]:
                                e.tensor_add(o, a, x),
                                waits=[tt1, txr[ff], t1_war[par]], sig=True)
                    t1_war[par] = mk.dma("Q", x1_d[ff * 128:(ff + 1) * 128, cs], X1T[par][:],
                                         waits=[tx1])
            mk.barrier()
            mk.flush("phaseA_oproj")


        ffn_ctr = [0]

        def ffn_phase(src, outd, quant, resid2=None, skip_ln=False):
            P_ = f"f{ffn_ctr[0]}"
            ffn_ctr[0] += 1
            if not skip_ln:
                with ExitStack() as ph:
                    XCH = [ph.enter_context(nc.sbuf_tensor(f"{P_}fx{i}", [128, TC], F32)) for i in range(NFT)]
                    XBF = [ph.enter_context(nc.sbuf_tensor(f"{P_}fb{i}", [128, TC], BF16)) for i in range(2)]
                    XSQ = [ph.enter_context(nc.sbuf_tensor(f"{P_}fq{i}", [128, TC], BF16)) for i in range(2)]
                    ln_pass(src, [x[:] for x in XCH], [x[:] for x in XBF], [x[:] for x in XSQ])
                    mk.barrier()
                    mk.flush(P_ + "ffn_ln")
            with ExitStack() as ph:
                if quant:
                    WI = [[ph.enter_context(nc.sbuf_tensor(f"{P_}fwi{w}{i}", [128, D], I32))
                           for i in range(2)] for w in range(2)]
                WB = [[ph.enter_context(nc.sbuf_tensor(f"{P_}fwb{w}{i}", [128, D], BF16))
                       for i in range(2)] for w in range(2)]
                SIL = [ph.enter_context(nc.sbuf_tensor(f"{P_}fsil{i}", [128, TC], BF16)) for i in range(2)]
                HJT = [ph.enter_context(nc.sbuf_tensor(f"{P_}fhjt{i}", [128, TC], BF16)) for i in range(2)]
                if quant:
                    GSU = ph.enter_context(nc.sbuf_tensor(P_ + "fgsu", [128, 2 * NJI], F32))
                    t_s = mk.dma("Q", GSU[:, 0:NJI], dr["gs"])
                    t_s2 = mk.dma("Q", GSU[:, NJI:2 * NJI], dr["us"])
                    mk.barrier()
                wsrc = (dr["gw"], dr["uw"]) if quant else (dr["mg"], dr["mu"])
                wi_war = [[None, None], [None, None]]
                wb_war = [[None, None], [None, None]]
                sil_war = [None, None]
                hjt_war = [None, None]
                for j in range(NJI):
                    par = j % 2
                    tdq = [None, None]
                    for w in range(2):
                        if quant:
                            tld = mk.dma("Q", WI[w][par][:], wsrc[w][j].rearrange("p a b -> p (a b)"),
                                         waits=[wi_war[w][par]] if wi_war[w][par] else [])
                            tdq[w] = mk.op("S", lambda e, o=WB[w][par][:], i=WI[w][par][:]:
                                           e.activation(o, i, AF.Copy, bias=-128.0),
                                           waits=[tld, wb_war[w][par]], sig=True)
                            wi_war[w][par] = tdq[w]
                        else:
                            tdq[w] = mk.dma("G", WB[w][par][:], wsrc[w][j].rearrange("p a b -> p (a b)"),
                                            waits=[wb_war[w][par]] if wb_war[w][par] else [])
                    g_last = None
                    u_last = None
                    for tc in range(NCH):
                        items_g = [(WB[0][par][:, ft * 128:(ft + 1) * 128],
                                    LNXs(ft)[:, tc * TC:(tc + 1) * TC], [tdq[0]]) for ft in range(NFT)]
                        gtok = mk.mm_group(PS[4][:], items_g, bank_tok=bank_free[4])
                        items_u = [(WB[1][par][:, ft * 128:(ft + 1) * 128],
                                    LNXs(ft)[:, tc * TC:(tc + 1) * TC], [tdq[1]]) for ft in range(NFT)]
                        utok = mk.mm_group(PS[5][:], items_u, bank_tok=bank_free[5])
                        spar = tc % 2
                        if quant:
                            tsil = mk.op("S", lambda e, o=SIL[spar][:], jj=j:
                                         e.activation(o, PS[4][:], AF.Silu, scale=GSU[:, jj:jj + 1]),
                                         waits=[gtok, sil_war[spar]], sig=True)
                        else:
                            tsil = mk.op("S", lambda e, o=SIL[spar][:]:
                                         e.activation(o, PS[4][:], AF.Silu),
                                         waits=[gtok, sil_war[spar]], sig=True)
                        bank_free[4] = tsil
                        if quant:
                            thj = mk.op("V", lambda e, o=HJT[spar][:], s=SIL[spar][:], jj=j:
                                        e.scalar_tensor_tensor(o, PS[5][:], GSU[:, NJI + jj:NJI + jj + 1],
                                                               s, ALU.mult, ALU.mult),
                                        waits=[utok, tsil, hjt_war[spar]], sig=True)
                        else:
                            thj = mk.op("V", lambda e, o=HJT[spar][:], s=SIL[spar][:]:
                                        e.tensor_tensor(o, PS[5][:], s, ALU.mult),
                                        waits=[utok, tsil, hjt_war[spar]], sig=True)
                        bank_free[5] = thj
                        sil_war[spar] = thj
                        g_last = gtok
                        u_last = utok
                        hjt_war[spar] = mk.dma("Q", hf_d[j * 128:(j + 1) * 128, tc * TC:(tc + 1) * TC],
                                               HJT[spar][:], waits=[thj])
                    wb_war[0][par] = g_last
                    wb_war[1][par] = u_last
                mk.barrier()
                mk.flush(P_ + "ffn_gateup")
            with ExitStack() as ph:
                XT = [ph.enter_context(nc.sbuf_tensor(f"{P_}fxt{i}", [128, TC], F32)) for i in range(NFT)]
                XT2 = [ph.enter_context(nc.sbuf_tensor(f"{P_}fxt2_{i}", [128, TC], F32)) for i in range(NFT)] if resid2 is not None else None
                if quant:
                    DWI = ph.enter_context(nc.sbuf_tensor(P_ + "fdwi", [128, FI], I32))
                DWB = [ph.enter_context(nc.sbuf_tensor(f"{P_}fdwb{i}", [128, FI], BF16)) for i in range(2)]
                T1 = [ph.enter_context(nc.sbuf_tensor(f"{P_}ft1_{i}", [128, TC], F32)) for i in range(2)]
                XO = [ph.enter_context(nc.sbuf_tensor(f"{P_}fxo{i}", [128, TC], F32)) for i in range(2)]
                if quant:
                    DSB = ph.enter_context(nc.sbuf_tensor(P_ + "fdsb", [128, NFT], F32))
                    mk.dma("Q", DSB[:], dr["ds"])
                    mk.barrier()
                dsrc = dr["dw"] if quant else dr["md"]

                def HT(kt):
                    return LNX[:, (kt // 4) * T + (kt % 4) * TC:(kt // 4) * T + (kt % 4 + 1) * TC]

                ht_war = [None] * NJI
                dwi_war = None
                dwb_war = [None, None]
                t1_war = [None, None]
                for tc in range(NCH):
                    th = []
                    for kt in range(NJI):
                        th.append(mk.dma("Q", HT(kt), hf_d[kt * 128:(kt + 1) * 128, tc * TC:(tc + 1) * TC],
                                         waits=[ht_war[kt]] if ht_war[kt] else []))
                    txr = [mk.dma("Q", XT[ft][:], src[ft * 128:(ft + 1) * 128, tc * TC:(tc + 1) * TC])
                           for ft in range(NFT)]
                    txr2 = [mk.dma("Q", XT2[ft][:], resid2[ft * 128:(ft + 1) * 128, tc * TC:(tc + 1) * TC])
                            for ft in range(NFT)] if resid2 is not None else None
                    for ff in range(NFT):
                        par = ff % 2
                        if quant:
                            tld = mk.dma("Q", DWI[:], dsrc[ff].rearrange("p a b -> p (a b)"),
                                         waits=[dwi_war] if dwi_war else [])
                            tdq = mk.op("S", lambda e, o=DWB[par][:], i=DWI[:]:
                                        e.activation(o, i, AF.Copy, bias=-128.0),
                                        waits=[tld, dwb_war[par]], sig=True)
                            dwi_war = tdq
                        else:
                            tdq = mk.dma("G", DWB[par][:], dsrc[ff].rearrange("p a b -> p (a b)"),
                                         waits=[dwb_war[par]] if dwb_war[par] else [])
                        items = [(DWB[par][:, kt * 128:(kt + 1) * 128], HT(kt), [tdq, th[kt]])
                                 for kt in range(NJI)]
                        gtok = mk.mm_group(PS[4 + par][:], items, bank_tok=bank_free[4 + par])
                        dwb_war[par] = gtok
                        for kt in range(NJI):
                            ht_war[kt] = gtok
                        if quant:
                            tt1 = mk.op("S", lambda e, o=T1[par][:], b=4 + par, f=ff:
                                        e.activation(o, PS[b][:], AF.Copy, scale=DSB[:, f:f + 1]),
                                        waits=[gtok, t1_war[par]], sig=True)
                        else:
                            tt1 = mk.op("S", lambda e, o=T1[par][:], b=4 + par:
                                        e.activation(o, PS[b][:], AF.Copy),
                                        waits=[gtok, t1_war[par]], sig=True)
                        bank_free[4 + par] = tt1
                        if resid2 is not None:
                            tmid = mk.op("V", lambda e, o=T1[par][:], a=T1[par][:], x=XT2[ff][:]:
                                         e.tensor_add(o, a, x),
                                         waits=[tt1, txr2[ff]], sig=True)
                        else:
                            tmid = tt1
                        txo = mk.op("V", lambda e, o=XO[par][:], a=T1[par][:], x=XT[ff][:]:
                                    e.tensor_add(o, a, x),
                                    waits=[tmid, txr[ff], t1_war[par]], sig=True)
                        t1_war[par] = mk.dma("Q", outd[ff * 128:(ff + 1) * 128, tc * TC:(tc + 1) * TC],
                                             XO[par][:], waits=[txo])
                mk.barrier()
                mk.flush(P_ + "ffn_down")

        if DEV_STAGE >= 2:
            ffn_phase(x1_d, x2_d, quant=True)

        def mamba_phase():
            with ExitStack() as ph:
                XCH = [ph.enter_context(nc.sbuf_tensor(f"mx{i}", [128, TC], F32)) for i in range(NFT)]
                XBF = [ph.enter_context(nc.sbuf_tensor(f"mb{i}", [128, TC], BF16)) for i in range(2)]
                XSQ = [ph.enter_context(nc.sbuf_tensor(f"mq{i}", [128, TC], BF16)) for i in range(2)]
                LXF = [ph.enter_context(nc.sbuf_tensor(f"ml{i}", [128, TC], F32)) for i in range(NFT)]
                RNT = [ph.enter_context(nc.sbuf_tensor(f"mr{i}", [128, TC], BF16)) for i in range(2)]
                LN1S = ph.enter_context(nc.sbuf_tensor("mln1", [128, NFT], F32))
                t_l1 = mk.dma("Q", LN1S[:], dr["ln1"])
                mk.barrier()
                xch_war = [[] for _ in range(NFT)]
                xbf_war = [None, None]
                xsq_war = [None, None]
                lxf_war = [[] for _ in range(NFT)]
                rnt_war = [None, None]
                row_war = [None, None, None]
                for tc in range(NCH):
                    cs = slice(tc * TC, (tc + 1) * TC)
                    tx = []
                    for ft in range(NFT):
                        tx.append(mk.dma("Q", XCH[ft][:], x2_d[ft * 128:(ft + 1) * 128, cs],
                                         waits=xch_war[ft]))
                        xch_war[ft] = []
                    tok_mean = None
                    tok_var = None
                    for ft in range(NFT):
                        pb = XBF[ft % 2][:]
                        tb = mk.op("V", lambda e, o=pb, i=XCH[ft][:]: e.tensor_copy(o, i),
                                   waits=[tx[ft], xbf_war[ft % 2]], sig=True)
                        tok_mean = mk.op("T", lambda e, r=pb, st=(ft == 0), sp=(ft == NFT - 1):
                                         e.matmul(PS[0][0:1, :], ONESC[:], r, start=st, stop=sp),
                                         waits=[tb] + ([bank_free[0]] if ft == 0 else []), sig=True)
                        xbf_war[ft % 2] = tok_mean
                        sq = XSQ[ft % 2][:]
                        ts_ = mk.op("S", lambda e, o=sq, i=XCH[ft][:]: e.activation(o, i, AF.Square),
                                    waits=[tx[ft], xsq_war[ft % 2]], sig=True)
                        tok_var = mk.op("T", lambda e, r=sq, st=(ft == 0), sp=(ft == NFT - 1):
                                        e.matmul(PS[1][0:1, :], ONESC[:], r, start=st, stop=sp),
                                        waits=[ts_] + ([bank_free[1]] if ft == 0 else []), sig=True)
                        xsq_war[ft % 2] = tok_var
                    mrow = ROWS[0:1, 0:TC]
                    vrow = ROWS[0:1, TC:2 * TC]
                    trow = ROWS[0:1, 2 * TC:3 * TC]
                    t1 = mk.op("S", lambda e: e.activation(mrow, PS[0][0:1, :], AF.Copy, scale=CVAL[0:1, 2:3]),
                               waits=[tok_mean, row_war[0]], sig=True)
                    t2 = mk.op("S", lambda e: e.activation(vrow, PS[1][0:1, :], AF.Copy, scale=CVAL[0:1, 2:3]),
                               waits=[tok_var, row_war[1]], sig=True)
                    t3 = mk.op("V", lambda e: e.tensor_mul(trow, mrow, mrow), waits=[t1, row_war[2]], sig=True)
                    t4 = mk.op("V", lambda e: e.tensor_sub(vrow, vrow, trow), waits=[t2, t3], sig=True)
                    t5 = mk.op("S", lambda e: e.activation(trow, vrow, AF.Sqrt, bias=CVAL[0:1, 0:1]),
                               waits=[t4], sig=True)
                    t6 = mk.op("V", lambda e: e.reciprocal(vrow, trow), waits=[t5], sig=True)
                    tm = mk.op("T", lambda e: e.matmul(PS[2][:, 0:TC], ONESR[:], mrow, start=True, stop=True),
                               waits=[t1, t3, bank_free[2]], sig=True)
                    tr = mk.op("T", lambda e: e.matmul(PS[3][:, 0:TC], ONESR[:], vrow, start=True, stop=True),
                               waits=[t6, bank_free[3]], sig=True)
                    tok_n = None
                    tok_ms = None
                    for ft in range(NFT):
                        tf = TMPF[:, (ft % 2) * TC:(ft % 2 + 1) * TC]
                        ta = mk.op("V", lambda e, o=tf, i=XCH[ft][:]: e.tensor_sub(o, i, PS[2][:, 0:TC]),
                                   waits=[tm, tx[ft]], sig=True)
                        xch_war[ft] = [ta, tok_mean, tok_var]
                        tok_n = mk.op("V", lambda e, o=LXF[ft][:], i=tf:
                                      e.tensor_tensor(o, i, PS[3][:, 0:TC], ALU.mult),
                                      waits=[tr, ta] + lxf_war[ft], sig=True)
                        tst = mk.dma("Q", lnx2_d[ft * 128:(ft + 1) * 128, cs], LXF[ft][:], waits=[tok_n])
                        sq = XSQ[ft % 2][:]
                        ts2 = mk.op("S", lambda e, o=sq, i=LXF[ft][:]: e.activation(o, i, AF.Square),
                                    waits=[tok_n, xsq_war[ft % 2]], sig=True)
                        tok_ms = mk.op("T", lambda e, r=sq, st=(ft == 0), sp=(ft == NFT - 1):
                                       e.matmul(PS[6][0:1, :], ONESC[:], r, start=st, stop=sp),
                                       waits=[ts2] + ([bank_free[6]] if ft == 0 else []), sig=True)
                        xsq_war[ft % 2] = tok_ms
                        lxf_war[ft] = [tst]
                    if DEV_TAP == "c1dbg" and tc == 0:
                        mk.barrier()
                        mk.dma("Q", y_out[0:128, 0:TC], XCH[0][:])
                        mk.dma("Q", y_out[128:256, 0:TC], LXF[0][:])
                        mk.dma("Q", y_out[256:257, 0:4 * TC], ROWS[0:1, :])
                        mk.dma("Q", y_out[384:512, 0:TC], TMPF[:, 0:TC])
                        tfresh = mk.dma("Q", XCH[1][:], x2_d[0:128, 0:TC])
                        mk.dma("Q", y_out[512:640, 0:TC], XCH[1][:], waits=[tfresh])
                        tfresh2 = mk.dma("Q", XCH[2][:], x1_d[0:128, 0:TC])
                        mk.dma("Q", y_out[640:768, 0:TC], XCH[2][:], waits=[tfresh2])
                        mk.barrier()
                        mk.flush("tap_c1dbg")
                        return nc
                    bank_free[0] = t1
                    bank_free[1] = t2
                    bank_free[2] = tok_n
                    bank_free[3] = tok_n
                    t7 = mk.op("S", lambda e: e.activation(mrow, PS[6][0:1, :], AF.Copy, scale=CVAL[0:1, 2:3]),
                               waits=[tok_ms, tm], sig=True)
                    bank_free[6] = t7
                    t8 = mk.op("S", lambda e: e.activation(trow, mrow, AF.Sqrt, bias=CVAL[0:1, 0:1]),
                               waits=[t7, tr], sig=True)
                    t9 = mk.op("V", lambda e: e.reciprocal(mrow, trow), waits=[t8], sig=True)
                    trr = mk.op("T", lambda e: e.matmul(PS[7][:, 0:TC], ONESR[:], mrow, start=True, stop=True),
                                waits=[t9, bank_free[7]], sig=True)
                    row_war = [t7, t4, t8]
                    tok_rn = None
                    for ft in range(NFT):
                        par = ft % 2
                        tok_rn = mk.op("V", lambda e, o=RNT[par][:], i=LXF[ft][:], f=ft:
                                       e.scalar_tensor_tensor(o, i, LN1S[:, f:f + 1], PS[7][:, 0:TC],
                                                              ALU.mult, ALU.mult),
                                       waits=[trr, tok_n, rnt_war[par], t_l1], sig=True)
                        lxf_war[ft].append(tok_rn)
                        rnt_war[par] = mk.dma("Q", rn1_d[ft * 128:(ft + 1) * 128, cs], RNT[par][:],
                                              waits=[tok_rn])
                    bank_free[7] = tok_rn
                mk.barrier()
                mk.flush("mamba_ln")

            with ExitStack() as ph:
                def sbp(name, shape, dtype):
                    return ph.enter_context(nc.sbuf_tensor(name, list(shape), dtype))
                PATP = sbp("patP_s", (128, 16 * 128), BF16)
                PATB = sbp("patB_s", (128, 64 * 128), BF16)
                PATQ = sbp("patQ_s", (128, 4 * 128), BF16)
                PA = sbp("pa_s", (128, NDT * DST), F32)
                CW = sbp("cw_s", (128, NDT * 4), F32)
                CB = sbp("cb_s", (128, NDT), F32)
                DTB = sbp("dtb_s", (128, NDT), F32)
                DPAR = sbp("dpar_s", (128, NDT), F32)
                DTW = sbp("dtw_s", (128, NDT * 128), BF16)

                DTL = sbp("dtl", (128, TC), BF16)
                XCONV = sbp("xconv", (128, NDT * 515), BF16)
                TAIL = sbp("tail", (128, NDT * 3), BF16)
                CACC = sbp("cacc", (128, TC), F32)
                STATE = sbp("state", (128, NDT * DST), F32)
                DA = sbp("da", (128, 2 * TC), F32)
                DBX = sbp("dbx", (128, 2 * TC), F32)
                HS = sbp("hs", (128, 2 * TC), F32)
                YT = sbp("yt", (128, TC), F32)
                YACC = sbp("yacc", (128, TC), F32)
                DTI = sbp("dti", (128, 2 * TC), F32)
                DTXI = sbp("dtxi", (128, 2 * TC), F32)
                SPT = sbp("spt", (128, TC), F32)
                RV = sbp("rv", (128, TC), F32)
                XCF = sbp("xcf", (128, TC), F32)
                IPS = sbp("ips", (128, D), BF16)
                OPS = sbp("ops", (128, DI), BF16)
                ZCS = [sbp(f"zcs{i}", (128, TC), BF16) for i in range(2)]
                CCS = [sbp(f"ccs{i}", (128, TC), BF16) for i in range(2)]
                ZLT = [sbp(f"zlt{i}", (128, TC), BF16) for i in range(2)]
                CLT = [sbp(f"clt{i}", (128, TC), BF16) for i in range(2)]
                SZ = sbp("sz", (128, TC), F32)
                LXT = [sbp(f"lxt{i}", (128, TC), F32) for i in range(2)]
                HMT = [sbp(f"hmt{i}", (128, TC), F32) for i in range(2)]

                def RN1C(ft):
                    return LNX[:, ft * T:ft * T + TC]

                def YF(i):
                    return LNX[:, (NFT // 2) * T + i * TC:(NFT // 2) * T + (i + 1) * TC]
                def YF(i):
                    ft = i % NFT
                    blk = 1 + i // NFT
                    return LNX[:, ft * T + blk * TC:ft * T + (blk + 1) * TC]

                def BOs(k):
                    return LNX[:, k * T + 3 * TC:(k + 1) * T]

                def XOs(k):
                    return LNX[:, (8 + k) * T + 3 * TC:(9 + k) * T]

                tc_c = []
                for s, d in [(PATP, "patP"), (PATB, "patB"), (PATQ, "patQ"), (PA, "alog"),
                             (CW, "cw"), (CB, "cb"), (DTB, "dtb"), (DPAR, "dpar")]:
                    tc_c.append(mk.dma("Q", s[:], dr[d]))
                tc_c.append(mk.dma("G", DTW[:], dr["dtw"]))
                tok_pa1 = mk.op("S", lambda e: e.activation(PA[:], PA[:], AF.Exp), waits=tc_c, sig=True)
                tok_pa = mk.op("V", lambda e: e.tensor_scalar_mul(PA[:], PA[:], -1.0), waits=[tok_pa1], sig=True)
                tz = mk.op("V", lambda e: e.memset(TAIL[:], 0.0), sig=True)
                mk.barrier()

                ips_war = None
                zcs_war = [None, None]
                ccs_war = [None, None]
                z_st = [None] * NDT
                c_st = [None] * NDT
                bo_war = [None] * 8
                xo_war = [None] * 8
                dtl_war = None
                rn1_war = [None] * NFT
                zlt_war = [None, None]
                clt_war = [None, None]
                yf_war = [None] * NDT
                lxt_war = [None, None]
                hmt_war = [None, None]
                ops_war = None
                xconv_war = [None] * NDT
                state_war = [None] * NDT
                for c in range(NCH):
                    cs = slice(c * TC, (c + 1) * TC)
                    trn = []
                    for ft in range(NFT):
                        trn.append(mk.dma("Q", RN1C(ft), rn1_d[ft * 128:(ft + 1) * 128, cs],
                                          waits=[rn1_war[ft]] if rn1_war[ft] else []))
                    for ot in range(81):
                        tld = mk.dma("G", IPS[:], dr["ip"][ot].rearrange("p a b -> p (a b)"),
                                     waits=[ips_war] if ips_war else [])
                        items = [(IPS[:, ft * 128:(ft + 1) * 128], RN1C(ft), [tld, trn[ft]])
                                 for ft in range(NFT)]
                        bk = 4 + ot % 2
                        gtok = mk.mm_group(PS[bk][:], items, bank_tok=bank_free[bk])
                        ips_war = gtok
                        if ot < 32:
                            par = ot % 2
                            tev = mk.op("S", lambda e, o=ZCS[par][:], b=bk:
                                        e.activation(o, PS[b][:], AF.Copy),
                                        waits=[gtok, zcs_war[par]], sig=True)
                            zcs_war[par] = mk.dma("Q", z_d[ot * 128:(ot + 1) * 128, cs], ZCS[par][:],
                                                  waits=[tev])
                            z_st[ot] = zcs_war[par]
                        elif ot < 40:
                            k = ot - 32
                            tev = mk.op("S", lambda e, o=XOs(k), b=bk:
                                        e.activation(o, PS[b][:], AF.Copy),
                                        waits=[gtok, xo_war[k]], sig=True)
                            xo_war[k] = tev
                        elif ot < 48:
                            k = ot - 40
                            tev = mk.op("S", lambda e, o=BOs(k), b=bk:
                                        e.activation(o, PS[b][:], AF.Copy),
                                        waits=[gtok, bo_war[k]], sig=True)
                            bo_war[k] = tev
                        elif ot < 80:
                            k = ot - 48
                            par = ot % 2
                            tev = mk.op("S", lambda e, o=CCS[par][:], b=bk:
                                        e.activation(o, PS[b][:], AF.Copy),
                                        waits=[gtok, ccs_war[par]], sig=True)
                            ccs_war[par] = mk.dma("Q", c_d[k * 128:(k + 1) * 128, cs], CCS[par][:],
                                                  waits=[tev])
                            c_st[k] = ccs_war[par]
                        else:
                            tev = mk.op("S", lambda e, o=DTL[:], b=bk:
                                        e.activation(o, PS[b][:], AF.Copy),
                                        waits=[gtok, dtl_war], sig=True)
                            tok_dtl = tev
                        bank_free[bk] = tev
                    for ft in range(NFT):
                        rn1_war[ft] = ips_war
                    conv_done = [None] * NDT
                    for i in range(NDT):
                        ttl = mk.op("V", lambda e, i=i:
                                    e.tensor_copy(XCONV[:, i * 515:i * 515 + 3],
                                                  TAIL[:, i * 3:(i + 1) * 3]),
                                    waits=[tz, xconv_war[i]], sig=True)
                        trep = mk.op("T", lambda e, i=i:
                                     e.matmul(PS[6][:], PATQ[:, (i % 4) * 128:(i % 4 + 1) * 128],
                                              XOs(i // 4),
                                              start=True, stop=True),
                                     waits=[xo_war[i // 4], bank_free[6], tok_pa], sig=True)
                        tev = mk.op("S", lambda e, i=i:
                                    e.activation(XCONV[:, i * 515 + 3:i * 515 + 515], PS[6][:], AF.Copy),
                                    waits=[trep, ttl, xconv_war[i]], sig=True)
                        bank_free[6] = tev
                        ttl2 = mk.op("V", lambda e, i=i:
                                     e.tensor_copy(TAIL[:, i * 3:(i + 1) * 3],
                                                   XCONV[:, i * 515 + 512:i * 515 + 515]),
                                     waits=[tev], sig=True)
                        tca = mk.op("V", lambda e, i=i:
                                    e.tensor_scalar(CACC[:], XCONV[:, i * 515:i * 515 + TC],
                                                    CW[:, i * 4:i * 4 + 1], None, ALU.mult),
                                    waits=[tev], sig=True)
                        for j in range(1, 4):
                            tca = mk.op("V", lambda e, i=i, j=j:
                                        e.scalar_tensor_tensor(CACC[:], XCONV[:, i * 515 + j:i * 515 + j + TC],
                                                               CW[:, i * 4 + j:i * 4 + j + 1], CACC[:],
                                                               ALU.mult, ALU.add),
                                        waits=[tca], sig=True)
                        tsl = mk.op("S", lambda e, i=i:
                                    e.activation(XCONV[:, i * 515 + 3:i * 515 + 515], CACC[:], AF.Silu,
                                                 bias=CB[:, i:i + 1]),
                                    waits=[tca, ttl2], sig=True)
                        conv_done[i] = tsl
                    yf_done = [None] * NDT
                    for i in range(NDT):
                        par = i % 2
                        tcc = mk.dma("Q", CLT[par][:], c_d[i * 128:(i + 1) * 128, cs],
                                     waits=([clt_war[par]] if clt_war[par] else []) + [c_st[i]])
                        tzl = mk.dma("Q", ZLT[par][:], z_d[i * 128:(i + 1) * 128, cs],
                                     waits=([zlt_war[par]] if zlt_war[par] else []) + [z_st[i]])
                        tdm = mk.op("T", lambda e, i=i:
                                    e.matmul(PS[6][:], DTW[:, i * 128:(i + 1) * 128], DTL[:],
                                             start=True, stop=True),
                                    waits=[tok_dtl, bank_free[6]], sig=True)
                        trv = mk.op("V", lambda e, i=i:
                                    e.tensor_scalar(RV[:], PS[6][:], DTB[:, i:i + 1], 0.0, ALU.add, ALU.max),
                                    waits=[tdm], sig=True)
                        tab = mk.op("S", lambda e, i=i:
                                    e.activation(SPT[:], PS[6][:], AF.Abs, bias=DTB[:, i:i + 1]),
                                    waits=[tdm], sig=True)
                        bank_free[6] = tab
                        tex = mk.op("S", lambda e: e.activation(SPT[:], SPT[:], AF.Exp, scale=-1.0),
                                    waits=[tab], sig=True)
                        tln = mk.op("S", lambda e: e.activation(SPT[:], SPT[:], AF.Ln, bias=1.0),
                                    waits=[tex], sig=True)
                        dti = DTI[:, par * TC:(par + 1) * TC]
                        tdt = mk.op("V", lambda e, o=dti: e.tensor_add(o, SPT[:], RV[:]),
                                    waits=[tln, trv], sig=True)
                        txc = mk.op("V", lambda e, i=i:
                                    e.tensor_copy(XCF[:], XCONV[:, i * 515 + 3:i * 515 + 515]),
                                    waits=[conv_done[i]], sig=True)
                        dtxi = DTXI[:, par * TC:(par + 1) * TC]
                        tdx = mk.op("V", lambda e, o=dtxi, a=dti: e.tensor_mul(o, a, XCF[:]),
                                    waits=[tdt, txc], sig=True)
                        tyac = None
                        hs_prev = None
                        for n in range(16):
                            np_ = n % 2
                            bbk = np_
                            cbk = 2 + np_
                            trb = mk.op("T", lambda e, i=i, n=n, b=bbk:
                                        e.matmul(PS[b][:], PATB[:, (n * 4 + i % 4) * 128:(n * 4 + i % 4 + 1) * 128],
                                                 BOs(i // 4),
                                                 start=True, stop=True),
                                        waits=[bo_war[i // 4], bank_free[bbk]], sig=True)
                            da = DA[:, np_ * TC:(np_ + 1) * TC]
                            tda = mk.op("S", lambda e, o=da, a=dti, i=i, n=n:
                                        e.activation(o, a, AF.Exp, scale=PA[:, i * DST + n:i * DST + n + 1]),
                                        waits=[tdt, hs_prev], sig=True)
                            dbx = DBX[:, np_ * TC:(np_ + 1) * TC]
                            tdb = mk.op("V", lambda e, o=dbx, a=dtxi, b=bbk:
                                        e.tensor_tensor(o, a, PS[b][:], ALU.mult),
                                        waits=[tdx, trb, hs_prev], sig=True)
                            bank_free[bbk] = tdb
                            hs = HS[:, np_ * TC:(np_ + 1) * TC]
                            if c == 0:
                                init = 0.0
                            else:
                                init = STATE[:, i * DST + n:i * DST + n + 1]
                            tsc = mk.op("V", lambda e, o=hs, a=da, b=dbx, ii=init:
                                        e.tensor_tensor_scan(o, a, b, ii, ALU.mult, ALU.add),
                                        waits=[tda, tdb, state_war[i]], sig=True)
                            tst = mk.op("V", lambda e, o=STATE[:, i * DST + n:i * DST + n + 1], h=hs:
                                        e.tensor_copy(o, h[:, TC - 1:TC]),
                                        waits=[tsc], sig=True)
                            trc = mk.op("T", lambda e, i=i, n=n, b=cbk:
                                        e.matmul(PS[b][:], PATP[:, n * 128:(n + 1) * 128], CLT[i % 2][:],
                                                 start=True, stop=True),
                                        waits=[tcc, bank_free[cbk]], sig=True)
                            tym = mk.op("V", lambda e, h=hs, b=cbk: e.tensor_tensor(YT[:], h, PS[b][:], ALU.mult),
                                        waits=[tsc, trc], sig=True)
                            bank_free[cbk] = tym
                            if n == 0:
                                tyac = mk.op("V", lambda e: e.tensor_copy(YACC[:], YT[:]),
                                             waits=[tym], sig=True)
                            else:
                                tyac = mk.op("V", lambda e: e.tensor_add(YACC[:], YACC[:], YT[:]),
                                             waits=[tym], sig=True)
                            hs_prev = tym
                        state_war[i] = tst
                        tg1 = mk.op("V", lambda e, i=i:
                                    e.scalar_tensor_tensor(YT[:], XCF[:], DPAR[:, i:i + 1], YACC[:],
                                                           ALU.mult, ALU.add),
                                    waits=[tyac, txc], sig=True)
                        tsz = mk.op("S", lambda e, p=par: e.activation(SZ[:], ZLT[p][:], AF.Silu),
                                    waits=[tzl], sig=True)
                        tyf = mk.op("V", lambda e, i=i, o=None: e.tensor_tensor(YF(i), YT[:], SZ[:], ALU.mult),
                                    waits=[tg1, tsz, yf_war[i]], sig=True)
                        yf_done[i] = tyf
                        clt_war[par] = tym
                        zlt_war[par] = tyf
                    for ff in range(NFT):
                        par = ff % 2
                        tld = mk.dma("G", OPS[:], dr["op"][ff].rearrange("p a b -> p (a b)"),
                                     waits=[ops_war] if ops_war else [])
                        items = [(OPS[:, i * 128:(i + 1) * 128], YF(i), [tld, yf_done[i]])
                                 for i in range(NDT)]
                        bk = 4 + par
                        gtok = mk.mm_group(PS[bk][:], items, bank_tok=bank_free[bk])
                        ops_war = gtok
                        for i in range(NDT):
                            yf_war[i] = gtok
                        tlx = mk.dma("Q", LXT[par][:], lnx2_d[ff * 128:(ff + 1) * 128, cs],
                                     waits=[lxt_war[par]] if lxt_war[par] else [])
                        thm = mk.op("V", lambda e, o=HMT[par][:], b=bk, lx=LXT[par][:]:
                                    e.tensor_add(o, PS[b][:], lx),
                                    waits=[gtok, tlx, hmt_war[par]], sig=True)
                        bank_free[bk] = thm
                        lxt_war[par] = thm
                        hmt_war[par] = mk.dma("Q", hm_d[ff * 128:(ff + 1) * 128, cs], HMT[par][:],
                                              waits=[thm])
                    for i in range(NDT):
                        xconv_war[i] = ips_war
                mk.barrier()
                mk.flush("mamba_core")

            if DEV_TAP == "mdbg":
                with ExitStack() as ph:
                    CPB = ph.enter_context(nc.sbuf_tensor("mdbgb", [128, T], BF16))
                    CPF = ph.enter_context(nc.sbuf_tensor("mdbgf", [128, T], F32))
                    rowmap = [(rn1_d, 0, BF16), (z_d, 128, BF16), (c_d, 256, BF16), (hm_d, 384, F32),
                              (z_d, 512, BF16), (c_d, 640, BF16)]
                    prev = None
                    for i, (srcd2, dst, dt_) in enumerate(rowmap):
                        srow = (i // 4) * 128 if srcd2 in (z_d, c_d) and dst >= 512 else 0
                        srow = 128 if dst >= 512 else 0
                        buf = CPB if dt_ == BF16 else CPF
                        tl = mk.dma("Q", buf[:], srcd2[srow:srow + 128, :], waits=[prev] if prev else [])
                        if dt_ == BF16:
                            prev = mk.dma("G", y_out[dst:dst + 128, :], buf[:], waits=[tl])
                        else:
                            prev = mk.dma("Q", y_out[dst:dst + 128, :], buf[:], waits=[tl])
                    mk.barrier()
                    mk.flush("tap_mdbg")
                return nc
            if DEV_TAP in ("hm", "rn1", "lnx2", "x2end"):
                srcd = {"hm": hm_d, "rn1": rn1_d, "lnx2": lnx2_d, "x2end": x2_d}[DEV_TAP]
                with ExitStack() as ph:
                    if DEV_TAP in ("hm", "lnx2", "x2end"):
                        CP = [ph.enter_context(nc.sbuf_tensor(f"tcp{i}", [128, T], F32)) for i in range(2)]
                    else:
                        CP = [ph.enter_context(nc.sbuf_tensor(f"tcp{i}", [128, T], BF16)) for i in range(2)]
                    war = [None, None]
                    for ft in range(NFT):
                        par = ft % 2
                        tl = mk.dma("Q", CP[par][:], srcd[ft * 128:(ft + 1) * 128, :],
                                    waits=[war[par]] if war[par] else [])
                        if DEV_TAP in ("hm", "lnx2", "x2end"):
                            war[par] = mk.dma("Q", y_out[ft * 128:(ft + 1) * 128, :], CP[par][:], waits=[tl])
                        else:
                            war[par] = mk.dma("Q", y_out[ft * 128:(ft + 1) * 128, :], CP[par][:], waits=[tl])
                    mk.barrier()
                    mk.flush("tap_hm")
                return nc

            with ExitStack() as ph:
                XCH = [ph.enter_context(nc.sbuf_tensor(f"hx{i}", [128, TC], F32)) for i in range(NFT)]
                XSQ = [ph.enter_context(nc.sbuf_tensor(f"hq{i}", [128, TC], BF16)) for i in range(2)]
                LN2S = ph.enter_context(nc.sbuf_tensor("hln2", [128, NFT], F32))
                t_l2 = mk.dma("Q", LN2S[:], dr["ln2"])
                mk.barrier()
                xch_war = [[] for _ in range(NFT)]
                xsq_war = [None, None]
                row_war = [None, None]
                for tc in range(NCH):
                    cs = slice(tc * TC, (tc + 1) * TC)
                    tx = []
                    for ft in range(NFT):
                        tx.append(mk.dma("Q", XCH[ft][:], hm_d[ft * 128:(ft + 1) * 128, cs],
                                         waits=xch_war[ft]))
                        xch_war[ft] = []
                    tok_ms = None
                    for ft in range(NFT):
                        sq = XSQ[ft % 2][:]
                        ts_ = mk.op("S", lambda e, o=sq, i=XCH[ft][:]: e.activation(o, i, AF.Square),
                                    waits=[tx[ft], xsq_war[ft % 2]], sig=True)
                        tok_ms = mk.op("T", lambda e, r=sq, st=(ft == 0), sp=(ft == NFT - 1):
                                       e.matmul(PS[0][0:1, :], ONESC[:], r, start=st, stop=sp),
                                       waits=[ts_] + ([bank_free[0]] if ft == 0 else []), sig=True)
                        xsq_war[ft % 2] = tok_ms
                    mrow = ROWS[0:1, 0:TC]
                    trow = ROWS[0:1, TC:2 * TC]
                    t1 = mk.op("S", lambda e: e.activation(mrow, PS[0][0:1, :], AF.Copy, scale=CVAL[0:1, 2:3]),
                               waits=[tok_ms, row_war[0]], sig=True)
                    bank_free[0] = t1
                    t2 = mk.op("S", lambda e: e.activation(trow, mrow, AF.Sqrt, bias=CVAL[0:1, 0:1]),
                               waits=[t1, row_war[1]], sig=True)
                    t3 = mk.op("V", lambda e: e.reciprocal(mrow, trow), waits=[t2], sig=True)
                    trr = mk.op("T", lambda e: e.matmul(PS[7][:, 0:TC], ONESR[:], mrow, start=True, stop=True),
                                waits=[t3, bank_free[7]], sig=True)
                    tok_rn = None
                    for ft in range(NFT):
                        tok_rn = mk.op("V", lambda e, o=LNXs(ft)[:, cs], i=XCH[ft][:], f=ft:
                                       e.scalar_tensor_tensor(o, i, LN2S[:, f:f + 1], PS[7][:, 0:TC],
                                                              ALU.mult, ALU.mult),
                                       waits=[trr, tx[ft], t_l2], sig=True)
                        xch_war[ft] = [tok_rn]
                    bank_free[7] = tok_rn
                    row_war = [t1, t2]
                mk.barrier()
                mk.flush("mamba_rms2")

            ffn_phase(hm_d, x3_d, quant=False, resid2=x2_d, skip_ln=True)

        def vis_read(dst_row, srcd, tag):
            with ExitStack() as ph:
                CP = ph.enter_context(nc.sbuf_tensor(f"vis{tag}", [128, TC], F32))
                CP2 = ph.enter_context(nc.sbuf_tensor(f"vis2{tag}", [128, TC], F32))
                CPB = ph.enter_context(nc.sbuf_tensor(f"visb{tag}", [128, TC], BF16))
                if srcd.tensor.dtype == F32:
                    tl = mk.dma("Q", CP[:], srcd[0:128, 0:TC])
                    tv = mk.op("V", lambda e: e.tensor_copy(CP2[:], CP[:]), waits=[tl], sig=True)
                    mk.dma("Q", y_out[dst_row:dst_row + 128, 0:TC], CP2[:], waits=[tv])
                else:
                    tl = mk.dma("Q", CPB[:], srcd[0:128, 0:TC])
                    mk.dma("G", y_out[dst_row:dst_row + 128, 0:TC], CPB[:], waits=[tl])
                mk.barrier()
                mk.flush(f"vis{tag}")

        if DEV_TAP == "vis":
            vis_read(896, x_in, "f")
            vis_read(0, x2_d, "a")
            vis_read(384, x1_d, "d")
            vis_read(512, hf_d, "e")

        if DEV_STAGE >= 3:
            mamba_phase()
            if DEV_TAP == "vis":
                vis_read(128, x2_d, "b")
                vis_read(256, lnx2_d, "c")
                mk.barrier()
                mk.flush("vis_end")
        if DEV_STAGE >= 4:
            ffn_phase(x3_d, y_out, quant=True)

        if DEV_TAP == "o":
            with ExitStack() as ph:
                CP = [ph.enter_context(nc.sbuf_tensor(f"cpo{i}", [128, T], BF16)) for i in range(2)]
                war = [None, None]
                for ft in range(NFT):
                    par = ft % 2
                    tl = mk.dma("Q", CP[par][:], o_d[ft * 128:(ft + 1) * 128, :],
                                waits=[war[par]] if war[par] else [])
                    war[par] = mk.dma("Q", y_out[ft * 128:(ft + 1) * 128, :], CP[par][:], waits=[tl])
                mk.barrier()
                mk.flush("tap_o")
            return nc
        src_final = {1: x1_d, 2: x2_d, 3: x3_d, 4: None}[DEV_STAGE]
        if DEV_TAP in ("vis", "mdbg", "hm", "rn1", "lnx2", "x2end"):
            src_final = None
        if src_final is not None:
            with ExitStack() as ph:
                CP = [ph.enter_context(nc.sbuf_tensor(f"cp{i}", [128, T], F32)) for i in range(2)]
                war = [None, None]
                for ft in range(NFT):
                    par = ft % 2
                    tl = mk.dma("Q", CP[par][:], src_final[ft * 128:(ft + 1) * 128, :],
                                waits=[war[par]] if war[par] else [])
                    war[par] = mk.dma("Q", y_out[ft * 128:(ft + 1) * 128, :], CP[par][:], waits=[tl])
                mk.barrier()
                mk.flush("final_copy")

    return nc



_nc_cache = {}


def kernel(**inputs):
    g, xs = _prep(inputs)
    nc = build_program()
    in_maps = []
    for c in range(NCORE):
        m = {k: v for k, v in g.items()}
        m["x"] = xs[c]
        in_maps.append(m)
    res = run_bass_kernel_spmd(nc, in_maps, core_ids=list(range(NCORE)))
    out = np.stack([np.ascontiguousarray(res.results[c]["y"].T) for c in range(NCORE)])
    return out.astype(np.float32)
